# version 05
# baseline (speedup 1.0000x reference)
import numpy as np
from contextlib import ExitStack
import ml_dtypes
import concourse.bass as bass
import concourse.mybir as mybir
from concourse.bass_utils import run_bass_kernel_spmd

F32 = mybir.dt.float32
BF16 = mybir.dt.bfloat16
AF = mybir.ActivationFunctionType
OP = mybir.AluOpType

P = 128
D = 1024
NCH = 8
NK = 16
SG = 128
NB = 4
NG = NB * 16
EPS = 1e-5
LATEDUMP = False

ENGINES = ("sync", "vector", "tensor", "scalar", "gpsimd")
SEMS = ("dmaS", "dmaG", "sV", "sT", "sSC", "sGP")


class Ctx:
    def __init__(self):
        self.thunks = {e: [] for e in ENGINES}
        self.cnt = {s: 0 for s in SEMS}
        self.sems = None

    def _push(self, eng, fn):
        self.thunks[eng].append(fn)

    def wait(self, eng, sem_name, value):
        if value <= 0:
            return
        def f(e, _s=sem_name, _v=value):
            e.wait_ge(self.sems[_s], _v)
        self._push(eng, f)

    def op(self, eng, fn, mark_sem=None, inc=1):
        if mark_sem is None:
            self._push(eng, fn)
            return None
        self.cnt[mark_sem] += inc
        def f(e, _fn=fn, _s=mark_sem, _i=inc):
            _fn(e).then_inc(self.sems[_s], _i)
        self._push(eng, f)
        return self.cnt[mark_sem]


def build_nc():
    nc = bass.Bass("TRN2", target_bir_lowering=False)

    x_d = nc.dram_tensor("x", [NB * D, 2048], F32, kind="ExternalInput").ap()
    y_d = nc.dram_tensor("y", [NB * D, 2048], F32, kind="ExternalOutput").ap()
    wqkvt_d = nc.dram_tensor("wqkvt", [D, 3072], F32, kind="ExternalInput").ap()
    wot_d = nc.dram_tensor("wot", [D, D], F32, kind="ExternalInput").ap()
    bkq_d = nc.dram_tensor("bkq", [P, 16], F32, kind="ExternalInput").ap()
    bv_d = nc.dram_tensor("bv", [1, D], F32, kind="ExternalInput").ap()
    boarr_d = nc.dram_tensor("boarr", [P, 8], F32, kind="ExternalInput").ap()
    windup_d = nc.dram_tensor("windup", [D, P], F32, kind="ExternalInput").ap()
    w1arr_d = nc.dram_tensor("w1arr", [P, 8], F32, kind="ExternalInput").ap()
    b1arr_d = nc.dram_tensor("b1arr", [P, 8], F32, kind="ExternalInput").ap()
    bindup_d = nc.dram_tensor("bindup", [P, 1], F32, kind="ExternalInput").ap()
    w2d_d = nc.dram_tensor("w2d", [P, 520], F32, kind="ExternalInput").ap()
    b2c_d = nc.dram_tensor("b2c", [65, 1], F32, kind="ExternalInput").ap()
    wout_d = nc.dram_tensor("wout", [64, D], F32, kind="ExternalInput").ap()
    boutarr_d = nc.dram_tensor("boutarr", [P, 8], F32, kind="ExternalInput").ap()
    ident_d = nc.dram_tensor("ident", [P, P], BF16, kind="ExternalInput").ap()

    with ExitStack() as _stack:
        def EC(x):
            return _stack.enter_context(x)
        wqkv_bf = EC(nc.sbuf_tensor("wqkv_bf", [P, NCH * 3072], BF16))
        wot_bf = EC(nc.sbuf_tensor("wot_bf", [P, NCH * 1024], BF16))
        windup_bf = EC(nc.sbuf_tensor("windup_bf", [P, NCH * 128], BF16))
        wout_bf = EC(nc.sbuf_tensor("wout_bf", [65, 1024], BF16))
        w2d_bf = EC(nc.sbuf_tensor("w2d_bf", [P, 520], BF16))
        bqkvv_bf = EC(nc.sbuf_tensor("bqkvv_bf", [P, 1024], BF16))
        bkq_f = EC(nc.sbuf_tensor("bkq_f", [P, 16], F32))
        boarr_f = EC(nc.sbuf_tensor("boarr_f", [P, 8], F32))
        w1arr_f = EC(nc.sbuf_tensor("w1arr_f", [P, 8], F32))
        b1eff_f = EC(nc.sbuf_tensor("b1eff_f", [P, 8], F32))
        b1tmp_f = EC(nc.sbuf_tensor("b1tmp_f", [P, 8], F32))
        bindup_f = EC(nc.sbuf_tensor("bindup_f", [P, 1], F32))
        b2c_f = EC(nc.sbuf_tensor("b2c_f", [65, 1], F32))
        boutarr_f = EC(nc.sbuf_tensor("boutarr_f", [P, 8], F32))
        ident_bf = EC(nc.sbuf_tensor("ident_bf", [P, P], BF16))
        ones65 = EC(nc.sbuf_tensor("ones65", [P, 65], BF16))
        bones = EC(nc.sbuf_tensor("bones", [65, P], BF16))
        bqvrow = EC(nc.sbuf_tensor("bqvrow", [65, 1024], BF16))
        epsT = EC(nc.sbuf_tensor("epsT", [1, 1], F32))
        expbias = EC(nc.sbuf_tensor("expbias", [P, 1], F32))
        b1fin = EC(nc.sbuf_tensor("b1fin", [P, 8], F32))
        stageA = EC(nc.sbuf_tensor("stageA", [P, 3072], F32))
        stageB = EC(nc.sbuf_tensor("stageB", [P, 3072], F32))
        xg2 = [EC(nc.sbuf_tensor(f"xg{i}", [P, 1024], F32)) for i in range(3)]
        xbsqA = EC(nc.sbuf_tensor("xbsqA", [P, 2048], BF16))
        xbsqC = EC(nc.sbuf_tensor("xbsqC", [P, 2048], BF16))
        xln2 = [EC(nc.sbuf_tensor(f"xln{i}", [P, 1024], BF16)) for i in range(3)]
        klo2 = [EC(nc.sbuf_tensor(f"klo{i}", [65, NK * SG], BF16)) for i in range(2)]
        qlo2 = [EC(nc.sbuf_tensor(f"qlo{i}", [65, NK * SG], BF16)) for i in range(2)]
        qhi2 = [EC(nc.sbuf_tensor(f"qhi{i}", [P, NK * SG], BF16)) for i in range(2)]
        vtok2 = [EC(nc.sbuf_tensor(f"vtok{i}", [P, NK * 65], BF16)) for i in range(2)]
        vtmp2 = [EC(nc.sbuf_tensor(f"vtmp{i}", [P, NK * 64], BF16)) for i in range(2)]
        probsQ = EC(nc.sbuf_tensor("probsQ", [P, 16 * 512], BF16))
        valsf = EC(nc.sbuf_tensor("valsf", [P, 1024], BF16))
        xn = EC(nc.sbuf_tensor("xn", [P, 1024], F32))
        xt1 = EC(nc.sbuf_tensor("xt1", [P, 1024], F32))
        yout = EC(nc.sbuf_tensor("yout", [P, 1024], F32))
        ht = EC(nc.sbuf_tensor("ht", [P, 1024], BF16))
        ht2 = EC(nc.sbuf_tensor("ht2", [P, 1024], BF16))
        sdup = EC(nc.sbuf_tensor("sdup", [P, 128], BF16))
        g65 = EC(nc.sbuf_tensor("g65", [65, 128], BF16))
        mvt = EC(nc.sbuf_tensor("mvt", [1, 256], F32))
        vart = EC(nc.sbuf_tensor("vart", [1, 128], F32))
        lnt = EC(nc.sbuf_tensor("lnt", [1, 128], F32))
        abrow = EC(nc.sbuf_tensor("abrow", [65, 256], BF16))
        zr = EC(nc.sbuf_tensor("zr", [P, 1], F32))
        absbA = EC(nc.sbuf_tensor("absbA", [P, 256], F32))
        absbC = EC(nc.sbuf_tensor("absbC", [P, 256], F32))
        psS = EC(nc.psum_tensor("psS", [P, 2048], F32))
        psB = EC(nc.psum_tensor("psB", [P, 512], F32))
        psC = EC(nc.psum_tensor("psC", [P, 1024], F32))
        psD = EC(nc.psum_tensor("psD", [P, 512], F32))
        sem_dmaS = EC(nc.semaphore("dmaS"))
        sem_dmaG = EC(nc.semaphore("dmaG"))
        sem_sV = EC(nc.semaphore("sV"))
        sem_sT = EC(nc.semaphore("sT"))
        sem_sSC = EC(nc.semaphore("sSC"))
        sem_sGP = EC(nc.semaphore("sGP"))
        block = EC(nc.Block())

        ctx = Ctx()
        ctx.sems = {"dmaS": sem_dmaS, "dmaG": sem_dmaG, "sV": sem_sV,
                    "sT": sem_sT, "sSC": sem_sSC, "sGP": sem_sGP}
        psB_bf = psB[:, :].bitcast(BF16)

        dbg_d = {}
        if LATEDUMP:
            for nm, shp, dt in [
                    ("dbg_ln3", [P, 1024], BF16), ("dbg_klo", [65, 2048], BF16),
                    ("dbg_qlo", [65, 2048], BF16), ("dbg_vtok", [P, 1040], BF16),
                    ("dbg_probs", [P, 8192], BF16), ("dbg_vtmp", [P, 1024], BF16),
                    ("dbg_valsf", [P, 1024], BF16), ("dbg_attno", [P, 2048], BF16),
                    ("dbg_xnew", [P, 1024], F32), ("dbg_ht", [P, 1024], BF16),
                    ("dbg_g65", [65, 128], BF16), ("dbg_sdup", [P, 128], BF16),
                    ("dbg_w1b", [P, 16], F32)]:
                dbg_d[nm] = nc.dram_tensor(nm, shp, dt, kind="ExternalOutput").ap()

        for fn in [
            lambda e: e.memset(ones65[:, :], 0.0),
            lambda e: e.memset(bones[:, :], 0.0),
            lambda e: e.memset(bqvrow[:, :], 0.0),
            lambda e: e.memset(abrow[:, :], 0.0),
            lambda e: e.memset(klo2[0][64:65, :], 0.0),
            lambda e: e.memset(klo2[1][64:65, :], 0.0),
            lambda e: e.memset(qlo2[0][64:65, :], 0.0),
            lambda e: e.memset(qlo2[1][64:65, :], 0.0),
            lambda e: e.memset(wout_bf[64:65, :], 0.0),
            lambda e: e.memset(epsT[:, :], EPS),
            lambda e: e.memset(expbias[:, :], -70.0),
            lambda e: e.memset(
                vtok2[0][:, :].rearrange("p (k d) -> p k d", d=65)[:, :, 64:65], 1.0),
            lambda e: e.memset(
                vtok2[1][:, :].rearrange("p (k d) -> p k d", d=65)[:, :, 64:65], 1.0),
            lambda e: e.memset(ones65[:, 0:1], 1.0),
            lambda e: e.memset(bones[0:1, :], 1.0),
        ]:
            ctx.op("vector", fn)

        def dma_mark(eng, fn):
            return ctx.op(eng, fn, mark_sem="dmaS", inc=16)

        cast_marks = []

        def stage_load(in_ap, width, out_ap, part=P):
            i = len(cast_marks)
            stage = stageA if i % 2 == 0 else stageB
            if i >= 2:
                ctx.wait("sync", "sV", cast_marks[i - 2])
            m = dma_mark("sync", lambda e, _s=stage, _i=in_ap, _w=width, _p=part:
                         e.dma_start(out=_s[0:_p, 0:_w], in_=_i))
            ctx.wait("vector", "dmaS", m)
            cm = ctx.op("vector",
                        lambda e, _s=stage, _w=width, _o=out_ap, _p=part:
                        e.tensor_copy(out=_o, in_=_s[0:_p, 0:_w]),
                        mark_sem="sV")
            cast_marks.append(cm)

        for c in range(NCH):
            stage_load(wqkvt_d[c * P:(c + 1) * P, :], 3072,
                       wqkv_bf[:, c * 3072:(c + 1) * 3072])
        for c in range(NCH):
            stage_load(wot_d[c * P:(c + 1) * P, :], 1024,
                       wot_bf[:, c * 1024:(c + 1) * 1024])
        for c in range(NCH):
            stage_load(windup_d[c * P:(c + 1) * P, :], 128,
                       windup_bf[:, c * 128:(c + 1) * 128])
        stage_load(wout_d, 1024, wout_bf[0:64, :], part=64)
        stage_load(w2d_d, 520, w2d_bf[:, :])
        for dst, src in [(bkq_f, bkq_d), (boarr_f, boarr_d), (w1arr_f, w1arr_d),
                         (b1tmp_f, b1arr_d), (bindup_f, bindup_d),
                         (b2c_f, b2c_d), (boutarr_f, boutarr_d),
                         (ident_bf, ident_d)]:
            dma_mark("sync", lambda e, _d=dst, _s=src:
                     e.dma_start(out=_d[:, :], in_=_s))
        ctx.wait("sync", "sV", cast_marks[-2])
        dma_mark("sync", lambda e: e.dma_start(out=stageA[0:1, 0:1024],
                                               in_=bv_d))
        setup_dma_total = ctx.cnt["dmaS"]
        ctx.wait("vector", "dmaS", setup_dma_total)
        ctx.op("vector", lambda e: e.tensor_copy(out=bqvrow[0:1, :],
                                                 in_=stageA[0:1, 0:1024]),
               mark_sem="sV")
        v_b1e = ctx.op("vector", lambda e: e.tensor_tensor(
            out=b1eff_f[:, :], in0=w1arr_f[:, :],
            in1=bindup_f[:, :].broadcast_to((P, 8)), op=OP.mult), mark_sem="sV")
        ctx.wait("vector", "sV", v_b1e)
        v_setup = ctx.op("vector", lambda e: e.tensor_tensor(
            out=b1fin[:, :], in0=b1eff_f[:, :], in1=b1tmp_f[:, :],
            op=OP.add), mark_sem="sV")
        ctx.wait("tensor", "sV", v_setup)
        for hh in range(2):
            ctx.op("tensor", lambda e, _h=hh: e.matmul(
                psC[:, _h * 512:(_h + 1) * 512], bones[:, :],
                bqvrow[:, _h * 512:(_h + 1) * 512], start=True, stop=True),
                mark_sem="sT")
        t_bv = ctx.cnt["sT"]
        ctx.wait("vector", "sT", t_bv)
        ctx.op("vector", lambda e: e.tensor_copy(out=bqkvv_bf[:, :],
                                                 in_=psC[:, 0:1024]),
               mark_sem="sV")

        M = [dict() for _ in range(NG)]
        state = {"psd_rd": 0, "psd_wr": 0, "psB_rd": 0, "psC_rd": 0}

        def bc(ap2, nmid):
            return ap2.unsqueeze(1).broadcast_to((ap2.shape[0], nmid, ap2.shape[1]))

        def bc_last(ap2, nlast):
            return ap2.unsqueeze(2).broadcast_to((ap2.shape[0], ap2.shape[1], nlast))

        def AB(name):
            t = absbA if name == "ln1" else absbC
            return t[:, 0:128], t[:, 128:256]

        def xb_slots(name):
            t = xbsqA if name == "ln1" else xbsqC
            return t[:, :].rearrange("p (c t) -> p c t", t=256)[:, :, 0:128]

        def sq_slots(name):
            t = xbsqA if name == "ln1" else xbsqC
            return t[:, :].rearrange("p (c t) -> p c t", t=256)[:, :, 128:256]

        def emit_stats(g, src_fn, cast_waits, name, reads_psC=False):
            for sem, val in cast_waits:
                ctx.wait("vector", sem, val)
            v_cast = ctx.op("vector", src_fn, mark_sem="sV")
            if reads_psC:
                state["psC_rd"] = max(state["psC_rd"], v_cast)
            v_sq = ctx.op("vector", lambda e, _n=name: e.tensor_tensor(
                out=sq_slots(_n), in0=xb_slots(_n), in1=xb_slots(_n),
                op=OP.mult), mark_sem="sV")
            M[g][name + "_sq"] = v_sq

        def emit_stats_mm(g, name):
            xbsq_t = xbsqA if name == "ln1" else xbsqC
            ctx.wait("tensor", "sV", M[g][name + "_sq"])
            ctx.wait("tensor", "sV", state["psd_rd"])
            for c in range(NCH):
                mk = "sT" if c == NCH - 1 else None
                ctx.op("tensor", lambda e, _c=c, _t=xbsq_t: e.matmul(
                    psD[0:65, 256:512], ones65[:, :],
                    _t[:, _c * 256:(_c + 1) * 256],
                    start=(_c == 0), stop=(_c == NCH - 1)), mark_sem=mk)
            M[g][name + "_stats"] = ctx.cnt["sT"]
            state["psd_wr"] = max(state["psd_wr"], ctx.cnt["sT"])

        def emit_chain(g, name):
            ctx.wait("vector", "sT", max(M[g][name + "_stats"],
                                          state["psd_wr"]))
            v_mv = ctx.op("vector", lambda e: e.tensor_scalar(
                out=mvt[:, :], in0=psD[0:1, 256:512], scalar1=1.0 / D,
                scalar2=None, op0=OP.mult), mark_sem="sV")
            state["psd_rd"] = max(state["psd_rd"], v_mv)
            ctx.wait("vector", "sV", v_mv)
            v_msq = ctx.op("vector", lambda e: e.tensor_tensor(
                out=lnt[:, :], in0=mvt[0:1, 0:128], in1=mvt[0:1, 0:128],
                op=OP.mult), mark_sem="sV")
            ctx.wait("vector", "sV", v_msq)
            v_var = ctx.op("vector", lambda e: e.tensor_tensor(
                out=vart[:, :], in0=mvt[0:1, 128:256], in1=lnt[:, :],
                op=OP.subtract), mark_sem="sV")
            ctx.wait("scalar", "sV", v_var)
            sc_ln = ctx.op("scalar", lambda e: e.activation(
                out=lnt[:, :], in_=vart[:, :], func=AF.Ln,
                bias=epsT[0:1, 0:1], scale=1.0), mark_sem="sSC")
            ctx.wait("scalar", "sSC", sc_ln)
            sc_a = ctx.op("scalar", lambda e: e.activation(
                out=abrow[0:1, 0:128], in_=lnt[:, :], func=AF.Exp, scale=-0.5),
                mark_sem="sSC")
            ctx.wait("vector", "sSC", sc_a)
            v_c = ctx.op("vector", lambda e: e.tensor_tensor(
                out=abrow[0:1, 128:256], in0=mvt[0:1, 0:128],
                in1=abrow[0:1, 0:128], op=OP.mult), mark_sem="sV")
            M[g][name + "_vc"] = v_c

        def emit_bcmm(g, name):
            ctx.wait("tensor", "sV", M[g][name + "_vc"])
            ctx.wait("tensor", "sV", state["psd_rd"])
            t_bc = ctx.op("tensor", lambda e: e.matmul(
                psD[:, 256:512], bones[:, :], abrow[:, :],
                start=True, stop=True), mark_sem="sT")
            M[g][name + "_bc"] = t_bc
            state["psd_wr"] = max(state["psd_wr"], t_bc)
            ctx.wait("vector", "sT", t_bc)
            _ab = absbA if name == "ln1" else absbC
            v_ac = ctx.op("vector", lambda e, _t=_ab: e.tensor_copy(
                out=_t[:, :], in_=psD[:, 256:512]), mark_sem="sV")
            state["psd_rd"] = max(state["psd_rd"], v_ac)
            ctx.wait("vector", "sV", v_ac)
            M[g][name + "_ac"] = v_ac

        def a_chunks(g):
            p = g % 2
            p3 = g % 3
            b, s0 = g // 16, (g % 16) * SG
            x3 = xg2[p3][:, :].rearrange("p (c s) -> p c s", c=8)
            xln3 = xln2[p3][:, :].rearrange("p (c s) -> p c s", c=8)
            xt3 = xt1[:, :].rearrange("p (c s) -> p c s", c=8)
            chunks = []

            def c_load():
                if g >= 3:
                    ctx.wait("sync", "sV", M[g - 3].get("v_y", 0))
                ctx.wait("sync", "dmaG", 16 * (g - 2))
                M[g]["x_dma"] = ctx.op("sync", lambda e: e.dma_start(
                    out=x3, in_=x_d[b * D:(b + 1) * D, s0:s0 + SG]
                    .rearrange("(c p) s -> p c s", c=8)), mark_sem="dmaS", inc=16)
            chunks.append(c_load)

            def c_cast1():
                waits = [("dmaS", M[g]["x_dma"])]
                if g >= 1:
                    waits.append(("sT", M[g - 1].get("ln1_stats", 0)))
                emit_stats(g, lambda e: e.tensor_copy(
                    out=xb_slots("ln1"), in_=x3), waits, "ln1")
            chunks.append(c_cast1)

            chunks.append(lambda: emit_stats_mm(g, "ln1"))

            chunks.append(lambda: emit_chain(g, "ln1"))

            def c_bcast1():
                emit_bcmm(g, "ln1")
                if g >= 3:
                    ctx.wait("vector", "sT", M[g - 3].get("sdup_mm", 0))
                A1, C1 = AB("ln1")
                ctx.op("vector", lambda e, _a=A1: e.tensor_tensor(
                    out=xt3, in0=x3, in1=bc(_a, 8), op=OP.mult), mark_sem="sV")
                v_xln = ctx.op("vector", lambda e, _c=C1: e.tensor_tensor(
                    out=xln3, in0=xt3, in1=bc(_c, 8), op=OP.subtract),
                    mark_sem="sV")
                M[g]["v_xln"] = v_xln
            chunks.append(c_bcast1)

            def mk_qk(blk):
                def c_qk():
                    ctx.wait("tensor", "sV", M[g]["v_xln"])
                    ctx.wait("tensor", "sV", state["psB_rd"])
                    if blk > 0:
                        ctx.wait("tensor", "sV", M[g]["qk_ev"])
                    if g >= 2:
                        ctx.wait("tensor", "sT", M[g - 2].get("fill_last", 0))
                        ctx.wait("tensor", "dmaS", M[g - 2].get("qdup", 0))
                    for kk in range(4):
                        k = blk * 4 + kk
                        for c in range(NCH):
                            mk = "sT" if (kk == 3 and c == NCH - 1) else None
                            ctx.op("tensor",
                                   lambda e, _k=k, _kk=kk, _c=c, _p3=p3:
                                   e.matmul(
                                psB[:, _kk * 128:(_kk + 1) * 128],
                                wqkv_bf[:, _c * 3072 + 192 * _k:
                                        _c * 3072 + 192 * _k + 128],
                                xln2[_p3][:, _c * 128:(_c + 1) * 128],
                                start=(_c == 0), stop=(_c == NCH - 1)),
                                mark_sem=mk)
                    t_blk = ctx.cnt["sT"]
                    ctx.wait("vector", "sT", t_blk)
                    ctx.op("vector", lambda e, _b=blk, _p=p: e.tensor_tensor(
                        out=klo2[_p][0:64, _b * 512:(_b + 1) * 512]
                        .rearrange("p (k s) -> p k s", k=4),
                        in0=psB[0:64, :].rearrange("p (k s) -> p k s", k=4),
                        in1=bc_last(bkq_f[0:64, _b * 4:(_b + 1) * 4], 128),
                        op=OP.add), mark_sem="sV")
                    M[g]["qk_ev"] = ctx.op("vector",
                                           lambda e, _b=blk, _p=p: e.tensor_tensor(
                        out=qhi2[_p][64:128, _b * 512:(_b + 1) * 512]
                        .rearrange("p (k s) -> p k s", k=4),
                        in0=psB[64:128, :].rearrange("p (k s) -> p k s", k=4),
                        in1=bc_last(bkq_f[64:128, _b * 4:(_b + 1) * 4], 128),
                        op=OP.add), mark_sem="sV")
                    state["psB_rd"] = max(state["psB_rd"], M[g]["qk_ev"])
                return c_qk
            for blk in range(4):
                chunks.append(mk_qk(blk))

            def mk_v(hh):
                def c_v():
                    ctx.wait("tensor", "sV", state["psC_rd"])
                    for c in range(NCH):
                        mk = "sT" if c == NCH - 1 else None
                        ctx.op("tensor", lambda e, _h=hh, _c=c, _p3=p3: e.matmul(
                            psC[:, _h * 512:(_h + 1) * 512],
                            xln2[_p3][:, _c * 128:(_c + 1) * 128],
                            wqkv_bf[:, _c * 3072:(_c + 1) * 3072]
                            .rearrange("p (k r) -> p k r", r=192)
                            [:, _h * 8:(_h + 1) * 8, 128:192],
                            start=(_c == 0), stop=(_c == NCH - 1)), mark_sem=mk)
                    t_v = ctx.cnt["sT"]
                    ctx.wait("vector", "sT", t_v)
                    if g >= 2 and hh == 0:
                        ctx.wait("vector", "sT", M[g - 2].get("attv_last", 0))
                    M[g]["v_ev"] = ctx.op("vector",
                                          lambda e, _h=hh, _p=p: e.tensor_tensor(
                        out=vtok2[_p][:, :].rearrange("p (k d) -> p k d", d=65)
                        [:, _h * 8:(_h + 1) * 8, 0:64],
                        in0=psC[:, _h * 512:(_h + 1) * 512]
                        .rearrange("p (k d) -> p k d", d=64),
                        in1=bqkvv_bf[:, _h * 512:(_h + 1) * 512]
                        .rearrange("p (k d) -> p k d", d=64),
                        op=OP.add), mark_sem="sV")
                    state["psC_rd"] = max(state["psC_rd"], M[g]["v_ev"])
                return c_v
            for hh in range(2):
                chunks.append(mk_v(hh))

            def c_qdup():
                ctx.wait("sync", "sV", M[g]["qk_ev"])
                if g >= 2:
                    ctx.wait("sync", "sT", M[g - 2].get("fill_last", 0))
                M[g]["qdup"] = ctx.op("sync", lambda e, _p=p: e.dma_start(
                    out=qlo2[_p][0:64, :], in_=qhi2[_p][64:128, :]),
                    mark_sem="dmaS", inc=16)
            chunks.append(c_qdup)
            return chunks

        def emit_b_fill(g, f):
            p = g % 2
            Q, hf = f // 8, f % 8
            half = f % 2
            ctx.wait("tensor", "dmaS", M[g]["qdup"])
            if f >= 2:
                ctx.wait("tensor", "sSC", M[g]["exp"][f - 2])
            for t in range(2):
                kj = 2 * hf + t
                mk = "sT" if t == 1 else None
                ctx.op("tensor", lambda e, _kj=kj, _t=t, _Q=Q, _h=half, _p=p:
                       e.matmul(
                    psS[:, _h * 1024 + _t * 512:_h * 1024 + (_t + 1) * 512],
                    klo2[_p][:, _kj * 128:(_kj + 1) * 128],
                    qlo2[_p][:, _Q * 512:(_Q + 1) * 512],
                    start=True, stop=True), mark_sem=mk)
            t_fill = ctx.cnt["sT"]
            M[g]["fill_last"] = t_fill
            ctx.wait("scalar", "sT", t_fill)
            M[g]["exp"][f] = ctx.op("scalar",
                                    lambda e, _hf=hf, _h=half: e.activation(
                out=probsQ[:, (2 * _hf) * 512:(2 * _hf) * 512 + 1024],
                in_=psS[:, _h * 1024:(_h + 1) * 1024], func=AF.Exp, scale=0.125,
                bias=expbias[:, :]), mark_sem="sSC")

        def emit_b_attv(g, Q, kiq):
            p = g % 2
            ki = Q * 4 + kiq
            slot = 0
            ctx.wait("tensor", "sSC", M[g]["exp"][Q * 8 + 7])
            if ki == 0:
                ctx.wait("tensor", "sV", M[g]["v_ev"])
            ctx.wait("tensor", "sV", state["psd_rd"])
            for kj in range(NK):
                mk = "sT" if kj == NK - 1 else None
                ctx.op("tensor", lambda e, _kj=kj, _kiq=kiq, _s=slot, _p=p:
                       e.matmul(
                    psD[:, _s:_s + 65],
                    probsQ[:, _kj * 512 + _kiq * 128:
                           _kj * 512 + _kiq * 128 + 128],
                    vtok2[_p][:, _kj * 65:(_kj + 1) * 65],
                    start=(_kj == 0), stop=(_kj == NK - 1)), mark_sem=mk)
            attv = ctx.cnt["sT"]
            M[g]["attv_last"] = attv
            state["psd_wr"] = max(state["psd_wr"], attv)
            ctx.wait("vector", "sT", attv)
            v_rec = ctx.op("vector", lambda e, _s=slot: e.reciprocal(
                out=zr[:, 0:1], in_=psD[:, _s + 64:_s + 65]), mark_sem="sV")
            ctx.wait("vector", "sV", v_rec)
            nm = ctx.op("vector", lambda e, _ki=ki, _s=slot, _p=p:
                        e.tensor_tensor(
                out=vtmp2[_p][:, _ki * 64:(_ki + 1) * 64],
                in0=psD[:, _s:_s + 64],
                in1=zr[:, 0:1].broadcast_to((P, 64)),
                op=OP.mult), mark_sem="sV")
            M[g]["norm"][ki] = nm
            state["psd_rd"] = max(state["psd_rd"], nm)

        def c_chunks(g):
            p = g % 2
            p3 = g % 3
            b, s0 = g // 16, (g % 16) * SG
            x3 = xg2[p3][:, :].rearrange("p (c s) -> p c s", c=8)
            xln3 = xln2[p3][:, :].rearrange("p (c s) -> p c s", c=8)
            xt3 = xt1[:, :].rearrange("p (c s) -> p c s", c=8)
            xn3 = xn[:, :].rearrange("p (c s) -> p c s", c=8)
            yout3 = yout[:, :].rearrange("p (c s) -> p c s", c=8)
            chunks = []

            def c_transp():
                ctx.wait("tensor", "sV", M[g]["norm"][15])
                ctx.wait("tensor", "sV", state["psB_rd"])
                ctx.op("tensor", lambda e: e.drain())
                for ki in range(NK):
                    mk = "sT" if ki == NK - 1 else None
                    ctx.op("tensor", lambda e, _ki=ki, _p=p: e.transpose(
                        psB_bf[(_ki % 2) * 64:(_ki % 2) * 64 + 64,
                               (_ki // 2) * 128:(_ki // 2) * 128 + 128],
                        vtmp2[_p][:, _ki * 64:(_ki + 1) * 64],
                        ident_bf[:, :]), mark_sem=mk)
                t_tr = ctx.cnt["sT"]
                M[g]["transp"] = t_tr
                ctx.op("tensor", lambda e: e.drain())
                ctx.wait("vector", "sT", t_tr)
                M[g]["v_vf"] = ctx.op("vector", lambda e: e.tensor_copy(
                    out=valsf[:, :], in_=psB_bf[:, :]), mark_sem="sV")
                state["psB_rd"] = max(state["psB_rd"], M[g]["v_vf"])
            chunks.append(c_transp)

            def mk_op(jj):
                def c_op():
                    if jj == 0:
                        ctx.wait("tensor", "sV", M[g]["v_vf"])
                        ctx.wait("tensor", "sV", state["psC_rd"])
                    for j in (2 * jj, 2 * jj + 1):
                        for c2 in range(NCH):
                            mk = ("sT" if (j == 2 * jj + 1 and c2 == NCH - 1)
                                  else None)
                            ctx.op("tensor", lambda e, _j=j, _c=c2: e.matmul(
                                psC[:, _j * 128:(_j + 1) * 128],
                                wot_bf[:, _c * 1024 + _j * 128:
                                       _c * 1024 + _j * 128 + 128],
                                valsf[:, _c * 128:(_c + 1) * 128],
                                start=(_c == 0), stop=(_c == NCH - 1)),
                                mark_sem=mk)
                    M[g]["t_op"] = ctx.cnt["sT"]
                return c_op
            for jj in range(4):
                chunks.append(mk_op(jj))

            chunks.append(lambda: emit_stats(g, lambda e: e.tensor_tensor(
                out=xb_slots("ln2"),
                in0=psC[:, :].rearrange("p (c s) -> p c s", c=8),
                in1=bc_last(boarr_f[:, :], 128), op=OP.add),
                [("sT", M[g]["t_op"])], "ln2", reads_psC=True))
            chunks.append(lambda: emit_stats_mm(g, "ln2"))

            chunks.append(lambda: emit_chain(g, "ln2"))

            def c_bcast2():
                emit_bcmm(g, "ln2")
                A2, C2 = AB("ln2")
                ctx.op("vector", lambda e, _a=A2: e.tensor_tensor(
                    out=xn3, in0=xb_slots("ln2"), in1=bc(_a, 8), op=OP.mult),
                    mark_sem="sV")
                ctx.op("vector", lambda e, _c=C2: e.tensor_tensor(
                    out=xt3, in0=xn3, in1=bc(_c, 8), op=OP.subtract),
                    mark_sem="sV")
                ctx.op("vector", lambda e: e.tensor_tensor(
                    out=xn3, in0=xt3, in1=x3, op=OP.add), mark_sem="sV")
            chunks.append(c_bcast2)

            chunks.append(lambda: emit_stats(
                g, lambda e: e.tensor_copy(out=xb_slots("ln3"), in_=xn3),
                [], "ln3"))
            chunks.append(lambda: emit_stats_mm(g, "ln3"))

            chunks.append(lambda: emit_chain(g, "ln3"))

            def c_bcast3():
                emit_bcmm(g, "ln3")
                A3, C3 = AB("ln3")
                ctx.op("vector", lambda e, _a=A3: e.tensor_tensor(
                    out=xt3, in0=xn3, in1=bc(_a, 8), op=OP.mult), mark_sem="sV")
                v_ln3 = ctx.op("vector", lambda e, _c=C3: e.tensor_tensor(
                    out=xln3, in0=xt3, in1=bc(_c, 8), op=OP.subtract),
                    mark_sem="sV")
                M[g]["v_ln3"] = v_ln3
            chunks.append(c_bcast3)

            def c_sdup():
                ctx.wait("tensor", "sV", M[g]["v_ln3"])
                ctx.wait("tensor", "sV", state["psB_rd"])
                for c in range(NCH):
                    mk = "sT" if c == NCH - 1 else None
                    ctx.op("tensor", lambda e, _c=c, _p3=p3: e.matmul(
                        psB[:, 0:128], windup_bf[:, _c * 128:(_c + 1) * 128],
                        xln2[_p3][:, _c * 128:(_c + 1) * 128],
                        start=(_c == 0), stop=(_c == NCH - 1)), mark_sem=mk)
                t_sd = ctx.cnt["sT"]
                M[g]["sdup_mm"] = t_sd
                ctx.wait("vector", "sT", t_sd)
                v_sd = ctx.op("vector", lambda e: e.tensor_copy(
                    out=sdup[:, :], in_=psB[:, 0:128]), mark_sem="sV")
                state["psB_rd"] = max(state["psB_rd"], v_sd)
                ctx.wait("vector", "sV", v_sd)
                for j in range(8):
                    ctx.op("vector", lambda e, _j=j: e.scalar_tensor_tensor(
                        out=ht[:, _j * 128:(_j + 1) * 128], in0=sdup[:, :],
                        scalar=w1arr_f[:, _j:_j + 1],
                        in1=b1fin[:, _j:_j + 1].broadcast_to((P, 128)),
                        op0=OP.mult, op1=OP.add), mark_sem="sV")
                if g >= 1:
                    ctx.wait("vector", "sT", M[g - 1].get("t_g", 0))
                M[g]["v_h"] = ctx.op("vector", lambda e: e.tensor_scalar(
                    out=ht2[:, :], in0=ht[:, :], scalar1=0.0, scalar2=None,
                    op0=OP.max), mark_sem="sV")
            chunks.append(c_sdup)

            def c_gmm():
                ctx.wait("tensor", "sV", M[g]["v_h"])
                ctx.wait("tensor", "sV", state["psB_rd"])
                for j in range(8):
                    mk = "sT" if j == 7 else None
                    ctx.op("tensor", lambda e, _j=j: e.matmul(
                        psB[0:65, 128:256], w2d_bf[:, _j * 65:(_j + 1) * 65],
                        ht2[:, _j * 128:(_j + 1) * 128],
                        start=(_j == 0), stop=(_j == 7)), mark_sem=mk)
                t_g = ctx.cnt["sT"]
                M[g]["t_g"] = t_g
                ctx.wait("vector", "sT", t_g)
                M[g]["v_g"] = ctx.op("vector", lambda e: e.tensor_scalar(
                    out=g65[:, :], in0=psB[0:65, 128:256], scalar1=b2c_f[:, :],
                    scalar2=None, op0=OP.add), mark_sem="sV")
                state["psB_rd"] = max(state["psB_rd"], M[g]["v_g"])
            chunks.append(c_gmm)

            def c_wout():
                ctx.wait("tensor", "sV", M[g]["v_g"])
                ctx.wait("tensor", "sV", state["psC_rd"])
                for j in range(8):
                    mk = "sT" if j == 7 else None
                    ctx.op("tensor", lambda e, _j=j: e.matmul(
                        psC[:, _j * 128:(_j + 1) * 128],
                        wout_bf[:, _j * 128:(_j + 1) * 128],
                        g65[:, :], start=True, stop=True), mark_sem=mk)
                M[g]["t_wo"] = ctx.cnt["sT"]
            chunks.append(c_wout)

            chunks.append(lambda: emit_stats(g, lambda e: e.tensor_tensor(
                out=xb_slots("ln4"),
                in0=psC[:, :].rearrange("p (c s) -> p c s", c=8),
                in1=bc_last(boutarr_f[:, :], 128), op=OP.add),
                [("sT", M[g]["t_wo"])], "ln4", reads_psC=True))

            def c_stats4():
                emit_stats_mm(g, "ln4")
                M[g]["ln4_stats"] = M[g]["ln4_stats"]
            chunks.append(c_stats4)

            chunks.append(lambda: emit_chain(g, "ln4"))

            def c_bcast4():
                emit_bcmm(g, "ln4")
                ctx.wait("vector", "dmaG", 16 * g)
                A4, C4 = AB("ln4")
                ctx.op("vector", lambda e, _a=A4: e.tensor_tensor(
                    out=xt3, in0=xb_slots("ln4"), in1=bc(_a, 8), op=OP.mult),
                    mark_sem="sV")
                ctx.op("vector", lambda e, _c=C4: e.tensor_tensor(
                    out=yout3, in0=xt3, in1=bc(_c, 8), op=OP.subtract),
                    mark_sem="sV")
                v_y = ctx.op("vector", lambda e: e.tensor_tensor(
                    out=x3, in0=yout3, in1=xn3, op=OP.add), mark_sem="sV")
                M[g]["v_y"] = v_y
                ctx.wait("gpsimd", "sV", v_y)
                ctx.op("gpsimd", lambda e: e.dma_start(
                    out=y_d[b * D:(b + 1) * D, s0:s0 + SG]
                    .rearrange("(c p) s -> p c s", c=8), in_=x3),
                    mark_sem="dmaG", inc=16)
            chunks.append(c_bcast4)
            return chunks

        for g in range(NG):
            M[g]["exp"] = {}
            M[g]["norm"] = {}
            M[g]["ln4_stats"] = 0

        for ch in a_chunks(0):
            ch()

        def build_queue(g):
            C = c_chunks(g - 1) if g >= 1 else None
            A = a_chunks(g + 1) if g + 1 < NG else None
            if C is None and A is None:
                return []
            if C is None:
                return A
            if A is None:
                return C
            order = [("c", 0), ("a", 0), ("c", 1), ("a", 1), ("c", 2),
                     ("a", 2), ("c", 3), ("a", 3), ("c", 4), ("a", 4),
                     ("c", 5), ("a", 5), ("c", 6), ("a", 6), ("c", 7),
                     ("a", 7), ("c", 8), ("a", 8), ("c", 9), ("a", 9),
                     ("c", 10), ("a", 10), ("c", 11), ("a", 11), ("c", 12),
                     ("c", 13), ("c", 14), ("c", 15), ("c", 16), ("c", 17),
                     ("c", 18), ("c", 19)]
            return [C[i] if t == "c" else A[i] for t, i in order]

        for g in range(NG):
            queue = build_queue(g)
            qi = [0]

            def pop():
                if qi[0] < len(queue):
                    queue[qi[0]]()
                    qi[0] += 1

            for Q in range(4):
                for hf in range(8):
                    emit_b_fill(g, Q * 8 + hf)
                    pop()
                    if hf % 2 == 1:
                        pop()
                for kiq in range(4):
                    emit_b_attv(g, Q, kiq)
                    pop()
            while qi[0] < len(queue):
                pop()

        for ch in c_chunks(NG - 1):
            ch()

        if LATEDUMP:
            pl = (NG - 1) % 2
            pl3 = (NG - 1) % 3
            for sem in ("sV", "sT", "sSC", "dmaS"):
                ctx.wait("gpsimd", sem, ctx.cnt[sem])
            for nm, src in [("dbg_ln3", xln2[pl3]), ("dbg_klo", klo2[pl]),
                            ("dbg_qlo", qlo2[pl]), ("dbg_vtok", vtok2[pl]),
                            ("dbg_probs", probsQ), ("dbg_vtmp", vtmp2[pl]),
                            ("dbg_valsf", valsf), ("dbg_attno", xbsqC),
                            ("dbg_xnew", xn), ("dbg_ht", ht2),
                            ("dbg_g65", g65), ("dbg_sdup", sdup)]:
                ctx.op("gpsimd", lambda e, _n=nm, _s=src: e.dma_start(
                    out=dbg_d[_n], in_=_s[:, :]), mark_sem="dmaG", inc=16)
            ctx.op("gpsimd", lambda e: e.dma_start(
                out=dbg_d["dbg_w1b"][:, 0:8], in_=w1arr_f[:, :]),
                mark_sem="dmaG", inc=16)
            ctx.op("gpsimd", lambda e: e.dma_start(
                out=dbg_d["dbg_w1b"][:, 8:16], in_=b1fin[:, :]),
                mark_sem="dmaG", inc=16)

        ctx.wait("gpsimd", "dmaG", ctx.cnt["dmaG"])

        @block.sync
        def _(e):
            for f in ctx.thunks["sync"]:
                f(e)

        @block.vector
        def _(e):
            for f in ctx.thunks["vector"]:
                f(e)

        @block.tensor
        def _(e):
            for f in ctx.thunks["tensor"]:
                f(e)

        @block.scalar
        def _(e):
            for f in ctx.thunks["scalar"]:
                f(e)

        @block.gpsimd
        def _(e):
            for f in ctx.thunks["gpsimd"]:
                f(e)

    return nc


def _host_inputs(inputs):
    x = np.asarray(inputs["L0_attn_x"])
    Wqkv = np.asarray(inputs["L0_attn_Wqkv"])
    bqkv = np.asarray(inputs["L0_attn_bqkv"])
    Wo = np.asarray(inputs["L0_attn_Wo"])
    bo = np.asarray(inputs["L0_attn_bo"])
    Win = np.asarray(inputs["L0_ffn_Win"])
    bin_ = np.asarray(inputs["L0_ffn_bin_"])
    W1 = np.asarray(inputs["L0_ffn_W1"])
    b1 = np.asarray(inputs["L0_ffn_b1"])
    W2 = np.asarray(inputs["L0_ffn_W2"])
    b2 = np.asarray(inputs["L0_ffn_b2"])
    Wout = np.asarray(inputs["L0_ffn_Wout"])
    bout = np.asarray(inputs["L0_ffn_bout"])

    x_feat = np.ascontiguousarray(x.transpose(0, 2, 1))

    WqkvT = Wqkv.T
    perm = np.empty(3072, dtype=np.int64)
    for k in range(16):
        perm[192 * k:192 * k + 64] = np.arange(192 * k + 64, 192 * k + 128)
        perm[192 * k + 64:192 * k + 128] = np.arange(192 * k, 192 * k + 64)
        perm[192 * k + 128:192 * k + 192] = np.arange(192 * k + 128,
                                                      192 * k + 192)
    wqkvt = np.ascontiguousarray(WqkvT[:, perm], dtype=np.float32)
    bkq = np.ascontiguousarray(
        bqkv[perm].reshape(16, 192)[:, 0:128].T, dtype=np.float32)
    bv = np.ascontiguousarray(
        bqkv.reshape(16, 192)[:, 128:192].reshape(1, 1024), dtype=np.float32)
    wot = np.ascontiguousarray(Wo.T, dtype=np.float32)
    boarr = np.ascontiguousarray(bo.reshape(8, 128).T, dtype=np.float32)
    windup = np.ascontiguousarray(
        np.concatenate([Win, Win], axis=1), dtype=np.float32)
    w1arr = np.empty((128, 8), dtype=np.float32)
    b1arr = np.empty((128, 8), dtype=np.float32)
    for i2 in range(2):
        for j in range(8):
            w1arr[i2 * 64:(i2 + 1) * 64, j] = W1[:, 2 * j + i2]
            b1arr[i2 * 64:(i2 + 1) * 64, j] = b1[:, 2 * j + i2]
    bindup = np.ascontiguousarray(
        np.concatenate([bin_, bin_]).reshape(128, 1), dtype=np.float32)
    w2d = np.zeros((128, 520), dtype=np.float32)
    m_idx = np.arange(64)
    for j in range(8):
        for i2 in range(2):
            w2d[i2 * 64 + m_idx, j * 65 + m_idx] = W2[:, 2 * j + i2]
    b2c = np.zeros((65, 1), dtype=np.float32)
    b2c[0:64, 0] = b2
    wout = np.ascontiguousarray(Wout, dtype=np.float32)
    boutarr = np.ascontiguousarray(bout.reshape(8, 128).T, dtype=np.float32)
    ident = np.eye(128, dtype=np.float32).astype(ml_dtypes.bfloat16)

    shared = {
        "wqkvt": wqkvt, "wot": wot, "bkq": bkq, "bv": bv, "boarr": boarr,
        "windup": windup, "w1arr": w1arr, "b1arr": b1arr, "bindup": bindup,
        "w2d": w2d, "b2c": b2c, "wout": wout, "boutarr": boutarr,
        "ident": ident,
    }
    in_maps = []
    for c in range(8):
        m = dict(shared)
        m["x"] = np.ascontiguousarray(
            x_feat[4 * c:4 * c + 4].reshape(4 * 1024, 2048), dtype=np.float32)
        in_maps.append(m)
    return in_maps


def kernel(**inputs):
    in_maps = _host_inputs(inputs)
    nc = build_nc()
    res = run_bass_kernel_spmd(nc, in_maps, core_ids=list(range(8)))
    out = np.empty((32, 1024, 2048), dtype=np.float32)
    for c in range(8):
        out[4 * c:4 * c + 4] = res.results[c]["y"].reshape(4, 1024, 2048)
    return np.ascontiguousarray(out.transpose(0, 2, 1))
